# version 01
# baseline (speedup 1.0000x reference)
import math
import numpy as np
import ml_dtypes
import concourse.bass as bass
import concourse.mybir as mybir
from concourse.bass_utils import run_bass_kernel_spmd

F32 = mybir.dt.float32
BF16 = mybir.dt.bfloat16
U8 = mybir.dt.uint8
AF = mybir.ActivationFunctionType
ALU = mybir.AluOpType
AXX = mybir.AxisListType.X

S, HID, NH, NKV, HD = 4096, 4096, 32, 8, 128
TOK = 1024
NKC = 32
NOC = 48
SC_ATT = 1.0 / math.sqrt(HD)
NSLOT, BOT, RDIM, VB = 1024, 64, 48, 256
EPS = 1e-5
GELU_A = 0.7978845608028654
GELU_B = 0.044715
LOG_PAD = math.log(SC_ATT * 1.04)
NEG = -1e30
PH = {"A": 0, "B": 1, "C": 2, "P1": 3, "P2": 4, "P3": 5, "P5": 6, "D": 9}


class Ctr:
    __slots__ = ("n",)
    def __init__(self):
        self.n = 0
    def inc(self, k=1):
        self.n += k
        return self.n


def build_nc(upto="D", debug_taps=(), no_cc=False):
    nc = bass.Bass("TRN2", target_bir_lowering=False, debug=False)
    lvl = PH[upto]

    def din(name, shape, dtype=F32):
        return nc.dram_tensor(name, shape, dtype, kind="ExternalInput").ap()

    x_fm = din("x_fm", [HID, TOK])
    wqkvT = din("wqkvT", [HID, 6144])
    woT = din("woT", [HID, HID])
    cos_fm = din("cos_fm", [HD, TOK])
    sin_fm = din("sin_fm", [HD, TOK])
    cmask_in = din("cmask_in", [128, 2048])
    ident_in = din("ident_in", [128, 128])
    avT_in = din("avT_in", [HID, NSLOT])
    wvdT_in = din("wvdT_in", [HID, VB])
    wvuT_in = din("wvuT_in", [VB, HID])
    wqfT_in = din("wqfT_in", [HID, BOT])
    wrT_in = din("wrT_in", [BOT, RDIM])
    akT_in = din("akT_in", [BOT, NSLOT])
    wu1T_in = din("wu1T_in", [HID, NSLOT])
    wu2pp_in = din("wu2pp_in", [128, 8])
    bu1pp_in = din("bu1pp_in", [128, 8])
    logrel_in = din("logrel_in", [1, NSLOT])
    scal_in = din("scal_in", [1, 16])
    y_out = nc.dram_tensor("y", [HID, TOK], F32, kind="ExternalOutput").ap()

    cc_kv_in = nc.dram_tensor("cc_kv_in", [2048, 1024], BF16)
    cc_kv_out = nc.dram_tensor("cc_kv_out", [8192, 1024], BF16)
    cc_mk_in = nc.dram_tensor("cc_mk_in", [1, 16], F32)
    cc_mk_out = nc.dram_tensor("cc_mk_out", [1, 16], F32)
    cc_lv_in = nc.dram_tensor("cc_lv_in", [1, 16], F32)
    cc_lv_out = nc.dram_tensor("cc_lv_out", [1, 16], F32)

    dbg = {}
    for name, shape, dtype in debug_taps:
        dbg[name] = nc.dram_tensor("dbg_" + name, shape, dtype, kind="ExternalOutput").ap()

    ARENA = 212832
    arena = nc.alloc_sbuf_tensor("arena", [128, ARENA], U8)
    base = nc.lookup_mloc(arena).addr

    def at(name, shape, dtype, off):
        nbytes = int(np.prod(shape[1:])) * mybir.dt.size(dtype)
        assert off + nbytes <= ARENA and off % 32 == 0, (name, off, nbytes)
        return nc.alloc_sbuf_tensor_at(name, shape, dtype, offset=base + off)

    R2, R3 = 131072, 196608
    xbf = at("xbf", [128, NKC * TOK], BF16, 0)
    q_fm = at("q_fm", [128, NH * TOK], BF16, 65536)
    x1 = at("x1", [128, NKC * TOK], F32, 0)
    kbuf = [at(f"kbuf{i}", [128, 4096], BF16, 8192 * i) for i in range(2)]
    vbuf = [at(f"vbuf{i}", [128, 4096], BF16, 16384 + 8192 * i) for i in range(2)]
    pTb = [at(f"pT{i}", [128, 1024], BF16, 32768 + 2048 * i) for i in range(2)]
    cmask = at("cmask", [128, 2048], BF16, 36864)
    rsum_sb = at("rsum_sb", [1, 512], F32, 40960)
    rb_sb = at("rb_sb", [128, 512], F32, 43008)
    cosb = at("cosb", [128, TOK], F32, R2)
    sinb = at("sinb", [128, TOK], F32, R2 + 4096)
    kown = at("kown", [128, 8 * TOK], BF16, R2 + 8192)
    vown = at("vown", [128, 8 * TOK], BF16, R2 + 24576)
    wbuf = [at(f"wbuf{i}", [128, 4096], BF16, R2 + 40960 + 8192 * i) for i in range(2)]
    rp1 = at("rp1", [128, TOK], F32, R2 + 57344)
    rp2 = at("rp2", [128, TOK], F32, R2 + 61440)
    attn_out = at("attn_out", [128, NH * TOK], BF16, R2)
    e_all = at("e_all", [128, 8 * 1024], BF16, R2)
    wvuTs = at("wvuTs", [128, 2 * 4096], BF16, R2)
    wsT = at("wsT", [128, 8 * 1024], BF16, R2 + 16384)
    logrelB = at("logrelB", [128, 1024], F32, R2 + 16384)
    s1p = at("s1p", [128, 1024], F32, R2 + 32768)
    s2p = at("s2p", [128, 1024], F32, R2 + 36864)
    sig2 = at("sig2", [1, 1024], F32, R2 + 32768)
    gtmp2 = at("gtmp2", [1, 1024], F32, R2 + 36864)
    gatef = at("gatef", [1, 1024], F32, R2 + 49152)
    mn2 = at("mn2", [1, 1024], F32, R2 + 32768)
    rs2 = at("rs2", [1, 1024], F32, R2 + 36864)
    x1bf = [at(f"x1bf{i}", [128, 512], BF16, R2 + 36864 + 1024 * i) for i in range(2)]
    gelu_b = at("gelu_b", [128, 8 * 512], BF16, R2 + 40960)
    avTb = [at(f"avTb{i}", [128, 1024], BF16, R2 + 40960 + 2048 * i) for i in range(2)]
    wvdTb = [at(f"wvdTb{i}", [128, 256], BF16, R2 + 45056 + 512 * i) for i in range(2)]
    aux2bf = at("aux2bf", [128, 2048], BF16, R2 + 49152)
    akTs = at("akTs", [64, NSLOT], BF16, R2 + 49152)
    rkb = at("rkb", [64, NSLOT], BF16, R2 + 51200)
    gelu_sc = [at(f"gelsc{i}", [128, 512], F32, R2 + 49152 + 2048 * i) for i in range(2)]
    wqfTs = at("wqfTs", [128, NKC * 64], BF16, R2 + 53248)
    xsqD = [at(f"xsqD{i}", [128, 512], BF16, R2 + 53248 + 1024 * i) for i in range(2)]
    wu1b = [at(f"wu1b{i}", [128, 1024], BF16, R2 + 57344 + 2048 * i) for i in range(2)]
    bauxp = at("bauxp", [128, 2048], BF16, R2 + 61440)
    woTb = [at(f"woTb{i}", [128, 1024], BF16, R3 + 2048 * i) for i in range(2)]
    mb_sb = at("mb_sb", [128, 512], F32, R3)
    rstdb_sb = at("rstdb_sb", [128, 512], F32, R3 + 2048)
    sqb = at("sqb", [128, TOK], BF16, R3 + 4096)
    vtmp = at("vtmp", [128, 1024], BF16, R3 + 6144)
    xbf_t = [at(f"xbf_t{i}", [128, 512], BF16, R3 + 4096 + 1024 * i) for i in range(2)]
    xsq_t = [at(f"xsq_t{i}", [128, 512], BF16, R3 + 6144 + 1024 * i) for i in range(2)]
    rqb = at("rqb", [64, NSLOT], BF16, R3 + 4096)
    queryb = at("queryb", [64, NSLOT], BF16, R3 + 6144)
    lvA = at("lvA", [1, 512], F32, R3 + 8192)
    lvB = at("lvB", [1, 512], F32, R3 + 10240)
    mnA = at("mnA", [1, 512], F32, R2 + 0)
    rsA = at("rsA", [1, 512], F32, R2 + 2048)
    mnB = at("mnB", [1, 512], F32, R2 + 4096)
    rsB = at("rsB", [1, 512], F32, R2 + 6144)
    PC = R3 + 12288
    identb = at("identb", [128, 128], BF16, PC)
    ones_bf = at("ones_bf", [128, 1], BF16, PC + 256)
    onesf32 = at("onesf32", [1, 128], F32, PC + 288)
    idf1 = at("idf1", [1, 1], F32, PC + 800)
    smalls = at("smalls", [1, 16], F32, PC + 832)
    pad16 = at("pad16", [1, 16], F32, PC + 896)
    mk2g = at("mk2g", [1, 16], F32, PC + 960)
    lvg = at("lvg", [1, 16], F32, PC + 1024)
    scal_sb = at("scal_sb", [1, 16], F32, PC + 1088)
    cbias = at("cbias", [128, 1], F32, PC + 1152)
    wu2s = at("wu2s", [128, 8], BF16, PC + 1184)
    bu1s = at("bu1s", [128, 8], F32, PC + 1216)
    wrTs = at("wrTs", [64, 48], BF16, PC + 1248)
    m8 = at("m8", [128, 8], F32, PC + 1376)
    nmx = at("nmx", [128, 1], F32, PC + 1408)
    f_t = at("f_t", [128, 1], F32, PC + 1440)
    sums_all = at("sums_all", [128, 8], F32, PC + 1472)
    gate_t = at("gate_t", [128, 8], F32, PC + 1504)
    scr_row = at("scr_row", [1, 512], F32, PC + 1536)
    c_eps = at("c_eps", [1, 1], F32, PC + 3584)
    c_one = at("c_one", [1, 1], F32, PC + 3616)
    c_logpad = at("c_logpad", [1, 1], F32, PC + 3648)
    assert PC + 3680 <= ARENA

    MQ2 = smalls[0:1, 0:1]
    MK2 = smalls[0:1, 1:2]
    SC1 = smalls[0:1, 2:3]
    SC2 = smalls[0:1, 3:4]
    CV = smalls[0:1, 4:5]
    LVR = smalls[0:1, 5:6]

    PSg = nc.psum_tensor("PS", [128, 4096], F32)
    PSh = PSg.__enter__()
    ps = PSh.ap()
    ps_bf = ps.bitcast(BF16)

    def bank(i, n=1):
        return ps[:, 512 * i: 512 * (i + n)]

    plans = {e: [] for e in ("t", "v", "s", "g", "y")}
    ctr = {k: Ctr() for k in ("LDG", "LDS", "STS", "CC", "PE", "DVE", "ACT", "GC")}
    sems = {}

    def E(q, fn):
        plans[q].append(fn)

    def W(q, sem, v):
        if v:
            E(q, lambda e, _s=sem, _v=v: e.wait_ge(sems[_s], _v))

    def dma(q, dst, src, sem, **kw):
        tgt = ctr[sem].inc(16)
        E(q, lambda e, d=dst, s=src, _m=sem, k=kw: e.dma_start(out=d, in_=s, **k).then_inc(sems[_m], 16))
        return tgt

    def op(q, fn, inc=None):
        if inc is None:
            E(q, fn)
            return None
        tgt = ctr[inc].inc(1)
        E(q, lambda e, f=fn, _m=inc: f(e).then_inc(sems[_m], 1))
        return tgt

    def coll(kind, alu, groups, ins, outs):
        tgt = ctr["CC"].inc(1)
        E("g", lambda e, k=kind, a=alu, g=groups, i=ins, o=outs:
          e.collective_compute(k, a, replica_groups=g, ins=i, outs=o).then_inc(sems["CC"], 1))
        return tgt

    G8 = [list(range(8))]
    G44 = [[0, 1, 2, 3], [4, 5, 6, 7]]

    taps_src = {}

    def finish():
        import contextlib
        W("y", "PE", ctr["PE"].n)
        W("y", "DVE", ctr["DVE"].n)
        W("y", "ACT", ctr["ACT"].n)
        W("y", "CC", ctr["CC"].n)
        W("y", "GC", ctr["GC"].n)
        W("y", "LDG", ctr["LDG"].n)
        last = ctr["STS"].n
        for name in dbg:
            last = dma("y", dbg[name], taps_src[name], "STS")
        W("y", "STS", last)
        with contextlib.ExitStack() as st:
            for k in ("LDG", "LDS", "STS", "CC", "PE", "DVE", "ACT", "GC"):
                sems[k] = st.enter_context(nc.semaphore(k.lower()))
            blk = st.enter_context(nc.Block())
            blk.tensor(lambda e: [f(e) for f in plans["t"]])
            blk.vector(lambda e: [f(e) for f in plans["v"]])
            blk.scalar(lambda e: [f(e) for f in plans["s"]])
            blk.gpsimd(lambda e: [f(e) for f in plans["g"]])
            blk.sync(lambda e: [f(e) for f in plans["y"]])
        return nc

    t_cos = dma("y", cosb[:, :], cos_fm, "LDS")
    t_sin = dma("y", sinb[:, :], sin_fm, "LDS")
    t_scal = dma("y", scal_sb[:, :], scal_in, "LDS")
    t_bu1 = dma("y", bu1s[:, :], bu1pp_in, "LDS")
    t_ident = dma("g", identb[:, :], ident_in, "LDG")
    t_wu2 = dma("g", wu2s[:, :], wu2pp_in, "LDG")
    t_wrT = dma("g", wrTs[:, :], wrT_in, "LDG")
    op("g", lambda e: e.memset(ones_bf[:, :], 1.0), inc="GC")
    op("g", lambda e: e.memset(onesf32[:, :], 1.0), inc="GC")
    op("g", lambda e: e.memset(idf1[:, :], 1.0), inc="GC")
    op("g", lambda e: e.memset(smalls[:, :], 0.0), inc="GC")
    op("g", lambda e: e.memset(c_eps[:, :], EPS), inc="GC")
    op("g", lambda e: e.memset(c_one[:, :], 1.0), inc="GC")
    op("g", lambda e: e.memset(c_logpad[:, :], LOG_PAD), inc="GC")
    t_gc0 = op("g", lambda e: e.memset(pad16[:, :], 0.0), inc="GC")
    t_xbf = []
    for i in range(4):
        src = x_fm[1024 * i:1024 * (i + 1), :].rearrange("(kc p) t -> p kc t", p=128)
        dst = xbf[:, 8192 * i:8192 * (i + 1)].rearrange("p (kc t) -> p kc t", kc=8)
        t_xbf.append(dma("g", dst, src, "LDG"))

    wslab_t = {}

    def load_wslab(oc, wait_pe=0):
        W("g", "PE", wait_pe)
        src = wqkvT[:, 128 * oc:128 * (oc + 1)].rearrange("(kc p) m -> p kc m", p=128)
        dst = wbuf[oc % 2].ap().rearrange("p (kc m) -> p kc m", kc=32)
        wslab_t[oc] = dma("g", dst, src, "LDG")

    load_wslab(0)
    load_wslab(1)

    pe_qkv, dve_rope, act_sq, pe_nrm, dve_nrm, pe_vt, dve_vevac = {}, {}, {}, {}, {}, {}, {}
    t_cc_mk = t_cc_kv = 0

    def emit_qkv(oc):
        pbuf = oc % 2
        prev = oc - 2
        if prev >= 0:
            W("t", "DVE", dve_rope.get(prev) or dve_vevac.get(prev))
        W("t", "LDG", max(wslab_t[oc], t_xbf[3]))
        if oc == 0:
            W("t", "GC", t_gc0)
        for half in range(2):
            for kc in range(NKC):
                def mm(e, _pb=pbuf, _h=half, _kc=kc, _oc=oc):
                    return e.matmul(bank(2 * _pb + _h),
                                    wbuf[_oc % 2][:, 128 * _kc:128 * (_kc + 1)],
                                    xbf[:, 1024 * _kc + 512 * _h: 1024 * _kc + 512 * _h + 512],
                                    start=(_kc == 0), stop=(_kc == NKC - 1))
                if half == 1 and kc == NKC - 1:
                    pe_qkv[oc] = op("t", mm, inc="PE")
                else:
                    op("t", mm)
        if oc + 2 < NOC:
            load_wslab(oc + 2, wait_pe=pe_qkv[oc])

    def emit_evac(oc):
        pbuf = oc % 2
        psqkv = ps[:, 1024 * pbuf:1024 * (pbuf + 1)]
        if oc < 8 or oc >= 16:
            dst = kown if oc < 8 else q_fm
            col = 1024 * oc if oc < 8 else 1024 * (oc - 16)
            lo, hi = psqkv[0:64, :], psqkv[64:128, :]
            W("v", "PE", pe_qkv[oc])
            if oc == 0:
                W("v", "LDS", max(t_cos, t_sin))
            op("v", lambda e, p=lo: e.tensor_mul(rp1[0:64, :], p, cosb[0:64, :]))
            op("v", lambda e, p=hi: e.tensor_mul(rp2[0:64, :], p, sinb[0:64, :]))
            op("v", lambda e, d=dst, c=col: e.tensor_sub(d[0:64, c:c + 1024], rp1[0:64, :], rp2[0:64, :]))
            op("v", lambda e, p=hi: e.tensor_mul(rp1[64:128, :], p, cosb[64:128, :]))
            op("v", lambda e, p=lo: e.tensor_mul(rp2[64:128, :], p, sinb[64:128, :]))
            dve_rope[oc] = op("v", lambda e, d=dst, c=col: e.tensor_add(d[64:128, c:c + 1024], rp1[64:128, :], rp2[64:128, :]), inc="DVE")
            last = max((o for o in pe_nrm if o < oc), default=None)
            W("s", "DVE", dve_rope[oc])
            if last is not None:
                W("s", "PE", pe_nrm[last])
            if oc == 16:
                W("s", "PE", pe_vt[15])
            act_sq[oc] = op("s", lambda e, d=dst, c=col: e.activation(sqb[:, :], d[:, c:c + 1024], AF.Square), inc="ACT")
            W("t", "ACT", act_sq[oc])
            if last is not None:
                W("t", "DVE", dve_nrm[last])
            op("t", lambda e: e.matmul(ps[0:1, 2048:2560], ones_bf[:, :], sqb[:, 0:512], start=True, stop=True))
            pe_nrm[oc] = op("t", lambda e: e.matmul(ps[0:1, 2560:3072], ones_bf[:, :], sqb[:, 512:1024], start=True, stop=True), inc="PE")
            run = MK2 if oc < 8 else MQ2
            W("v", "PE", pe_nrm[oc])
            op("v", lambda e: e.reduce_max(SC1, ps[0:1, 2048:3072], axis=AXX))
            dve_nrm[oc] = op("v", lambda e, r=run: e.tensor_max(r, r, SC1), inc="DVE")
        else:
            W("v", "PE", pe_qkv[oc])
            if oc - 1 in pe_vt:
                W("v", "PE", pe_vt[oc - 1])
            cp = op("v", lambda e, p=psqkv: e.tensor_copy(vtmp[:, :], p), inc="DVE")
            W("t", "DVE", cp)
            if oc - 1 in dve_vevac:
                W("t", "DVE", dve_vevac[oc - 1])
            if oc == 8:
                W("t", "LDG", t_ident)
            for g in range(8):
                def tp(e, _g=g):
                    return e.transpose(ps_bf[:, 6144 + 128 * _g: 6272 + 128 * _g],
                                       vtmp[:, 128 * _g:128 * (_g + 1)], identb[:, :])
                if g == 7:
                    pe_vt[oc] = op("t", tp, inc="PE")
                else:
                    op("t", tp)
            W("v", "PE", pe_vt[oc])
            vo = vown.ap().rearrange("p (g t) -> p g t", g=8)[:, :, 128 * (oc - 8):128 * (oc - 7)]
            pv = ps_bf[:, 6144:7168].rearrange("p (g t) -> p g t", g=8)
            dve_vevac[oc] = op("v", lambda e, d=vo, s_=pv: e.tensor_copy(d, s_), inc="DVE")

    for oc in range(NOC):
        emit_qkv(oc)
        if oc >= 1:
            emit_evac(oc - 1)
        if oc == 17 and not no_cc:
            t_mkpad = op("v", lambda e: e.tensor_scalar(pad16[0:1, 0:1], MK2, 1.04, None, ALU.mult), inc="DVE")
            W("y", "DVE", max(dve_rope[7], dve_vevac[15], t_mkpad))
            ko = cc_kv_in.ap()[0:1024, :].rearrange("(o p) t -> p o t", p=128)
            dma("y", ko, kown.ap().rearrange("p (o t) -> p o t", o=8), "STS")
            vo2 = cc_kv_in.ap()[1024:2048, :].rearrange("(g p) d -> p g d", p=128)
            dma("y", vo2, vown.ap().rearrange("p (g d) -> p g d", g=8), "STS")
            t_mk = dma("y", cc_mk_in.ap(), pad16[:, :], "STS")
    emit_evac(NOC - 1)
    W("g", "STS", t_mk)
    W("g", "LDG", ctr["LDG"].n)
    t_cc_mk = coll("AllReduce", ALU.max, G8, [cc_mk_in.ap()], [cc_mk_out.ap()])
    t_cc_kv = coll("AllGather", ALU.bypass, G44, [cc_kv_in.ap()], [cc_kv_out.ap()])

    if no_cc:
        op("v", lambda e: e.tensor_copy(mk2g[0:1, 0:1], MK2), inc="DVE")
    else:
        W("y", "CC", t_cc_kv)
    if not no_cc:
        t_mk2g = dma("y", mk2g[:, :], cc_mk_out.ap(), "LDS")
        W("v", "LDS", t_mk2g)
    t_c2 = op("v", lambda e: e.tensor_mul(SC2, MQ2, mk2g[0:1, 0:1]), inc="DVE")
    W("s", "DVE", t_c2)
    op("s", lambda e: e.activation(SC1, SC2, AF.Ln))
    t_cv = op("s", lambda e: e.activation(CV, SC1, AF.Exp, bias=c_logpad[0:1, 0:1], scale=0.5), inc="ACT")
    W("t", "ACT", t_cv)
    W("t", "DVE", dve_nrm[47])
    t_cmm = op("t", lambda e: e.matmul(ps[:, 3584:3585], onesf32[:, :], CV, start=True, stop=True), inc="PE")
    W("v", "PE", t_cmm)
    t_cbias = op("v", lambda e: e.tensor_scalar(cbias[:, :], ps[:, 3584:3585], -1.0, None, ALU.mult), inc="DVE")

    taps_src.update({
        "q_fm": q_fm[:, :], "kown": kown[:, :], "vown": vown[:, :],
        "cbias": cbias[:, :], "smalls": smalls[:, :], "mk2g": mk2g[:, :],
        "pad16": pad16[:, :],
    })
    if lvl == 0:
        return finish()

    W("g", "CC", t_cc_kv)
    t_cmask = dma("g", cmask[:, :], cmask_in, "LDG")
    kv_t = {}

    def load_kv(kvh, wait_pe=0):
        W("y", "CC", t_cc_kv)
        W("y", "PE", wait_pe)
        par = kvh % 2
        last = 0
        for r in range(4):
            src = cc_kv_out.ap()[2048 * r + 128 * kvh: 2048 * r + 128 * (kvh + 1), :]
            last = dma("y", kbuf[par][:, 1024 * r:1024 * (r + 1)], src, "LDS")
        for r in range(4):
            src = cc_kv_out.ap()[2048 * r + 1024: 2048 * r + 2048, 128 * kvh:128 * (kvh + 1)]
            src = src.rearrange("(g p) d -> p g d", p=128)
            dst = vbuf[par][:, 1024 * r:1024 * (r + 1)].rearrange("p (g d) -> p g d", g=8)
            last = dma("y", dst, src, "LDS")
        kv_t[kvh] = last

    load_kv(0)
    load_kv(1)

    qv = q_fm.ap().rearrange("p (h t) -> p h t", h=NH)
    avw = attn_out.ap().rearrange("p (h t) -> p h t", h=NH)

    gg = 0
    slot_free = [0, 0]
    slot_free_act = [0, 0]
    pT_free = [0, 0]
    par_free = [0, 0]
    pe_kvh_done = {}

    for kvh in range(NKV):
        kpar = kvh % 2
        for g in range(8):
            nkb = 4 * g + 4
            par = (kvh * 8 + g) % 2
            avbank = bank(4 + par)
            sumrow = ps[0:1, 512 * (6 + par):512 * (6 + par) + 512]
            qrhs = qv[:, 4 * kvh:4 * kvh + 4, 128 * g:128 * (g + 1)]
            W("t", "DVE", par_free[par])
            if g == 0:
                W("t", "LDS", kv_t[kvh])
                if kvh == 0:
                    W("t", "DVE", t_cbias)
                    W("t", "LDG", t_cmask)
            t_av = 0
            for pair in range(nkb // 2):
                slot = gg % 2
                gg += 1
                W("t", "ACT", slot_free_act[slot])
                W("t", "DVE", slot_free[slot])
                t_sc = 0
                for j in range(2):
                    kb = 2 * pair + j
                    col = (kb % 4) * 1024 + (kb // 4) * 128
                    def smm(e, _s=slot, _j=j, _c=col, _kp=kpar, _q=qrhs):
                        return e.matmul(bank(2 * _s + _j), kbuf[_kp][:, _c:_c + 128], _q,
                                        start=True, stop=True)
                    if j == 1:
                        t_sc = op("t", smm, inc="PE")
                    else:
                        op("t", smm)
                W("s", "PE", max(t_sc, pT_free[slot]))
                t_exp = op("s", lambda e, _s=slot: e.activation(
                    pTb[_s][:, :], ps[:, 1024 * _s:1024 * _s + 1024], AF.Exp,
                    bias=cbias[:, :], scale=SC_ATT), inc="ACT")
                slot_free_act[slot] = t_exp
                t_msk = 0
                for j in range(2):
                    kb = 2 * pair + j
                    d = kb - 4 * g
                    if d >= 0:
                        if not t_msk:
                            W("v", "ACT", t_exp)
                        t_msk = op("v", lambda e, _s=slot, _j=j, _d=d: e.tensor_mul(
                            pTb[_s][:, 512 * _j:512 * _j + 512],
                            pTb[_s][:, 512 * _j:512 * _j + 512],
                            cmask[:, 512 * _d:512 * _d + 512]), inc="DVE")
                W("t", "ACT", t_exp)
                if t_msk:
                    W("t", "DVE", t_msk)
                for j in range(2):
                    kb = 2 * pair + j
                    col = (kb % 4) * 1024 + (kb // 4) * 128
                    st_, sp_ = (kb == 0), (kb == nkb - 1)
                    op("t", lambda e, _s=slot, _j=j, _sr=sumrow, _st=st_, _sp=sp_: e.matmul(
                        _sr, ones_bf[:, :], pTb[_s][:, 512 * _j:512 * _j + 512],
                        start=_st, stop=_sp))
                    def avmm(e, _s=slot, _j=j, _c=col, _kp=kpar, _ab=avbank, _st=st_, _sp=sp_):
                        return e.matmul(_ab, vbuf[_kp][:, _c:_c + 128],
                                        pTb[_s][:, 512 * _j:512 * _j + 512], start=_st, stop=_sp)
                    if j == 1:
                        t_av = op("t", avmm, inc="PE")
                    else:
                        op("t", avmm)
                pT_free[slot] = t_av
            W("v", "PE", t_av)
            t_rcp = op("v", lambda e, _sr=sumrow: e.reciprocal(rsum_sb[:, :], _sr), inc="DVE")
            rbslot = gg % 2
            gg += 1
            W("t", "DVE", max(t_rcp, slot_free[rbslot]))
            W("t", "ACT", slot_free_act[rbslot])
            t_rb = op("t", lambda e, _s=rbslot: e.matmul(
                bank(2 * _s), onesf32[:, :], rsum_sb[:, :], start=True, stop=True), inc="PE")
            W("v", "PE", t_rb)
            op("v", lambda e, _s=rbslot: e.tensor_copy(rb_sb[:, :], bank(2 * _s)))
            aw = avw[:, 4 * kvh:4 * kvh + 4, 128 * g:128 * (g + 1)]
            avr = avbank.rearrange("p (h t) -> p h t", h=4)
            rbr = rb_sb.ap().rearrange("p (h t) -> p h t", h=4)
            t_evac = op("v", lambda e, _o=aw, _a=avr, _r=rbr: e.tensor_mul(_o, _a, _r), inc="DVE")
            par_free[par] = t_evac
            slot_free[rbslot] = t_evac
        pe_kvh_done[kvh] = ctr["PE"].n
        if kvh + 2 < NKV:
            load_kv(kvh + 2, wait_pe=pe_kvh_done[kvh])

    t_attn_dve = ctr["DVE"].n
    taps_src["attn_out"] = attn_out[:, :]
    if lvl == 1:
        return finish()

    wo_t = {}

    def load_wo(seq, wait_pe=0):
        m, qtr = divmod(seq, 4)
        W("g", "PE", wait_pe)
        src = woT[1024 * qtr:1024 * (qtr + 1), 128 * m:128 * (m + 1)]
        src = src.rearrange("(kc p) o -> p kc o", p=128)
        dst = woTb[seq % 2].ap().rearrange("p (kc o) -> p kc o", kc=8)
        wo_t[seq] = dma("g", dst, src, "LDG")

    load_wo(0)
    load_wo(1)

    x1ld = {}
    for m in range(NKC):
        for s_ in range(2):
            if m == 0 and s_ == 0:
                W("y", "DVE", t_attn_dve)
            x1ld[(m, s_)] = dma("y", x1[:, 1024 * m + 512 * s_:1024 * m + 512 * s_ + 512],
                                x_fm[128 * m:128 * (m + 1), 512 * s_:512 * (s_ + 1)], "LDS")

    pe_woq = {}
    pe_wo_s = {}
    pe_stat = {}
    dve_x1 = {}

    for m in range(NKC):
        for qtr in range(4):
            seq = 4 * m + qtr
            W("t", "LDG", wo_t[seq])
            if seq == 0:
                W("t", "DVE", t_attn_dve)
            if qtr == 0 and m >= 2:
                W("t", "DVE", max(dve_x1[(m - 2, 0)], dve_x1[(m - 2, 1)]))
            for s_ in range(2):
                for k8 in range(8):
                    kc = 8 * qtr + k8
                    st_ = (qtr == 0 and k8 == 0)
                    sp_ = (qtr == 3 and k8 == 7)
                    def wmm(e, _m=m, _s=s_, _k8=k8, _seq=seq, _kc=kc, _st=st_, _sp=sp_):
                        return e.matmul(bank(2 * (_m % 2) + _s),
                                        woTb[_seq % 2][:, 128 * _k8:128 * _k8 + 128],
                                        attn_out[:, 1024 * _kc + 512 * _s:1024 * _kc + 512 * _s + 512],
                                        start=_st, stop=_sp)
                    if s_ == 1 and k8 == 7:
                        pe_woq[seq] = op("t", wmm, inc="PE")
                        if sp_:
                            pe_wo_s[(m, 1)] = pe_woq[seq]
                    elif sp_:
                        pe_wo_s[(m, 0)] = op("t", wmm, inc="PE")
                    else:
                        op("t", wmm)
            if seq + 2 < 4 * NKC:
                load_wo(seq + 2, wait_pe=pe_woq[seq])
        for s_ in range(2):
            W("v", "PE", pe_wo_s[(m, s_)])
            W("v", "LDS", x1ld[(m, s_)])
            x1sl = x1[:, 1024 * m + 512 * s_: 1024 * m + 512 * s_ + 512]
            dve_x1[(m, s_)] = op("v", lambda e, _o=x1sl, _m=m, _s=s_: e.tensor_add(
                _o, _o, bank(2 * (_m % 2) + _s)), inc="DVE")
            W("v", "PE", pe_stat.get((m - 1, s_), 0))
            t_cast = op("v", lambda e, _o=x1sl, _s=s_: e.tensor_copy(xbf_t[_s][:, :], _o), inc="DVE")
            W("s", "PE", pe_stat.get((m - 1, s_), 0))
            W("s", "DVE", dve_x1[(m, s_)])
            t_sq = op("s", lambda e, _o=x1sl, _s=s_: e.activation(xsq_t[_s][:, :], _o, AF.Square), inc="ACT")
            W("t", "DVE", t_cast)
            W("t", "ACT", t_sq)
            op("t", lambda e, _m=m, _s=s_: e.matmul(
                ps[0:1, 512 * (4 + 2 * _s):512 * (4 + 2 * _s) + 512], ones_bf[:, :],
                xbf_t[_s][:, :], start=(_m == 0), stop=(_m == NKC - 1)))
            pe_stat[(m, s_)] = op("t", lambda e, _m=m, _s=s_: e.matmul(
                ps[0:1, 512 * (5 + 2 * _s):512 * (5 + 2 * _s) + 512], ones_bf[:, :],
                xsq_t[_s][:, :], start=(_m == 0), stop=(_m == NKC - 1)), inc="PE")

    t_fin = {}
    for s_, (mean_t, rstd_t, lv_t) in ((0, (mnA, rsA, lvA)), (1, (mnB, rsB, lvB))):
        mean_r, rstd_r, lv_r = mean_t[0:1, :], rstd_t[0:1, :], lv_t[0:1, :]
        W("v", "PE", pe_stat[(NKC - 1, s_)])
        op("v", lambda e, _m=mean_r, _s=s_: e.tensor_scalar(_m, ps[0:1, 512 * (4 + 2 * _s):512 * (4 + 2 * _s) + 512], 1.0 / HID, None, ALU.mult))
        op("v", lambda e, _l=lv_r, _s=s_: e.tensor_scalar(_l, ps[0:1, 512 * (5 + 2 * _s):512 * (5 + 2 * _s) + 512], 1.0 / HID, None, ALU.mult))
        op("v", lambda e, _r=rstd_r, _m=mean_r: e.tensor_mul(_r, _m, _m))
        t_var = op("v", lambda e, _l=lv_r, _r=rstd_r: e.tensor_sub(_l, _l, _r), inc="DVE")
        W("s", "DVE", t_var)
        op("s", lambda e, _r=rstd_r, _l=lv_r: e.activation(_r, _l, AF.Ln, bias=c_eps[0:1, 0:1]))
        t_rstd = op("s", lambda e, _r=rstd_r: e.activation(_r, _r, AF.Exp, scale=-0.5), inc="ACT")
        W("v", "ACT", t_rstd)
        op("v", lambda e, _l=lv_r: e.tensor_scalar(scr_row[:, :], _l, EPS, None, ALU.add))
        op("v", lambda e: e.reciprocal(scr_row[:, :], scr_row[:, :]))
        t_vh = op("v", lambda e, _l=lv_r: e.tensor_mul(_l, _l, scr_row[:, :]), inc="DVE")
        W("s", "DVE", t_vh)
        t_lv = op("s", lambda e, _l=lv_r: e.activation(_l, _l, AF.Ln, bias=c_one[0:1, 0:1]), inc="ACT")
        W("v", "ACT", t_lv)
        op("v", lambda e, _l=lv_r: e.reduce_sum(SC1, _l, axis=AXX))
        t_fin[s_] = op("v", lambda e: e.tensor_add(LVR, LVR, SC1), inc="DVE")
    t_lvp = op("v", lambda e: e.tensor_copy(pad16[0:1, 0:1], LVR), inc="DVE")
    W("y", "DVE", t_lvp)
    t_lvs = dma("y", cc_lv_in.ap(), pad16[:, :], "STS")
    W("g", "STS", t_lvs)
    W("g", "LDG", ctr["LDG"].n)
    t_cc_lv = coll("AllReduce", ALU.add, G8, [cc_lv_in.ap()], [cc_lv_out.ap()])
    W("y", "CC", t_cc_lv)
    t_lvg = dma("y", lvg[:, :], cc_lv_out.ap(), "LDS")

    t_ln1 = {}
    t_mbmm = {}
    t_bc_cp = 0
    for s_, (mean_t, rstd_t) in ((0, (mnA, rsA)), (1, (mnB, rsB))):
        W("t", "DVE", t_fin[s_])
        if s_ == 1:
            W("t", "DVE", t_bc_cp)
        t_mbmm[s_] = op("t", lambda e, _st=mean_t: e.matmul(bank(0), onesf32[:, :], _st[0:1, :], start=True, stop=True), inc="PE")
        t_rb2 = op("t", lambda e, _st=rstd_t: e.matmul(bank(1), onesf32[:, :], _st[0:1, :], start=True, stop=True), inc="PE")
        W("v", "PE", t_rb2)
        op("v", lambda e: e.tensor_copy(mb_sb[:, :], bank(0)))
        t_bc_cp = op("v", lambda e: e.tensor_copy(rstdb_sb[:, :], bank(1)), inc="DVE")
        for m in range(NKC):
            x1sl = x1[:, 1024 * m + 512 * s_: 1024 * m + 512 * s_ + 512]
            op("v", lambda e, _o=x1sl: e.tensor_sub(_o, _o, mb_sb[:, :]))
            t_ln1[(m, s_)] = op("v", lambda e, _o=x1sl: e.tensor_mul(_o, _o, rstdb_sb[:, :]), inc="DVE")

    avT_t, wvdT_t = {}, {}

    def load_av(kc, wait_pe=0):
        W("g", "PE", wait_pe)
        avT_t[kc] = dma("g", avTb[kc % 2][:, :], avT_in[128 * kc:128 * (kc + 1), :], "LDG")
        wvdT_t[kc] = dma("g", wvdTb[kc % 2][:, :], wvdT_in[128 * kc:128 * (kc + 1), :], "LDG")

    W("g", "CC", t_cc_lv)
    load_av(0)
    load_av(1)
    pe_baux = {}
    W("t", "DVE", t_bc_cp)
    for kc in range(NKC):
        W("t", "LDG", max(avT_t[kc], wvdT_t[kc]))
        for sc in range(8):
            def bmm(e, _kc=kc, _sc=sc):
                return e.matmul(ps[:, 512 * _sc:512 * _sc + 256],
                                avTb[_kc % 2][:, 128 * _sc:128 * (_sc + 1)],
                                wvdTb[_kc % 2][:, :], start=(_kc == 0), stop=(_kc == NKC - 1))
            if sc == 7:
                pe_baux[kc] = op("t", bmm, inc="PE")
            else:
                op("t", bmm)
        if kc + 2 < NKC:
            load_av(kc + 2, wait_pe=pe_baux[kc])
    W("v", "PE", pe_baux[NKC - 1])
    bsrc = ps.rearrange("p (b c) -> p b c", b=8)[:, :, 0:256]
    bdst = bauxp.ap().rearrange("p (b c) -> p b c", b=8)
    t_baux = op("v", lambda e: e.tensor_copy(bdst, bsrc), inc="DVE")

    taps_src.update({"x1": x1[:, :], "lvA": lvA[:, :], "lvB": lvB[:, :],
                     "baux": bauxp[:, :]})
    if lvl == 2:
        return finish()

    W("y", "PE", max(t_mbmm[0], t_mbmm[1]))
    t_lrA = dma("y", mnA[0:1, :], logrel_in[:, 0:512], "LDS")
    t_lrB = dma("y", mnB[0:1, :], logrel_in[:, 512:1024], "LDS")
    t_akT = dma("g", akTs[:, :], akT_in, "LDG")
    srcq = wqfT_in.rearrange("(kc p) o -> p kc o", p=128)
    t_wqf = dma("g", wqfTs.ap().rearrange("p (kc o) -> p kc o", kc=32), srcq, "LDG")
    W("t", "DVE", t_baux)
    W("t", "LDS", max(t_lrA, t_lrB))
    op("t", lambda e: e.matmul(bank(0), onesf32[:, :], mnA[0:1, :], start=True, stop=True))
    t_lb = op("t", lambda e: e.matmul(bank(1), onesf32[:, :], mnB[0:1, :], start=True, stop=True), inc="PE")
    W("v", "PE", t_lb)
    op("v", lambda e: e.tensor_copy(logrelB[:, 0:512], bank(0)))
    t_lrb = op("v", lambda e: e.tensor_copy(logrelB[:, 512:1024], bank(1)), inc="DVE")

    W("t", "LDG", max(t_akT, t_wrT))
    W("t", "DVE", t_lrb)
    op("t", lambda e: e.matmul(ps[0:48, 1024:1536], wrTs[0:64, :], akTs[0:64, 0:512], start=True, stop=True))
    t_rk = op("t", lambda e: e.matmul(ps[0:48, 1536:2048], wrTs[0:64, :], akTs[0:64, 512:1024], start=True, stop=True), inc="PE")
    W("v", "PE", t_rk)
    op("v", lambda e: e.tensor_copy(rkb[0:48, 0:512], ps[0:48, 1024:1536]))
    t_rkb = op("v", lambda e: e.tensor_copy(rkb[0:48, 512:1024], ps[0:48, 1536:2048]), inc="DVE")

    pe_q = {}
    for s_ in range(2):
        for kc in range(NKC):
            seq = 32 * s_ + kc
            tpar = seq % 2
            W("v", "PE", pe_q.get(seq - 2, 0))
            W("v", "DVE", t_ln1[(kc, s_)])
            t_cast = op("v", lambda e, _kc=kc, _s=s_, _p=tpar: e.tensor_copy(
                x1bf[_p][:, :], x1[:, 1024 * _kc + 512 * _s:1024 * _kc + 512 * _s + 512]), inc="DVE")
            W("t", "DVE", t_cast)
            if seq == 0:
                W("t", "LDG", t_wqf)
            pe_q[seq] = op("t", lambda e, _kc=kc, _p=tpar: e.matmul(
                ps[0:64, 2048:2560], wqfTs[:, 64 * _kc:64 * (_kc + 1)], x1bf[_p][:, :],
                start=(_kc == 0), stop=(_kc == NKC - 1)), inc="PE")
        W("v", "PE", pe_q[32 * s_ + NKC - 1])
        op("v", lambda e, _s=s_: e.tensor_copy(queryb[0:64, 512 * _s:512 * _s + 512], ps[0:64, 2048:2560]), inc="DVE")
    t_qev = ctr["DVE"].n
    W("t", "DVE", t_qev)
    op("t", lambda e: e.matmul(ps[0:48, 1024:1536], wrTs[0:64, :], queryb[0:64, 0:512], start=True, stop=True))
    t_rq = op("t", lambda e: e.matmul(ps[0:48, 1536:2048], wrTs[0:64, :], queryb[0:64, 512:1024], start=True, stop=True), inc="PE")
    W("v", "PE", t_rq)
    op("v", lambda e: e.tensor_copy(rqb[0:48, 0:512], ps[0:48, 1024:1536]))
    t_rqb = op("v", lambda e: e.tensor_copy(rqb[0:48, 512:1024], ps[0:48, 1536:2048]), inc="DVE")

    if lvl == 3:
        taps_src["queryb"] = queryb[:, :]
        return finish()

    dve_p2 = {}
    act_p2 = {}
    for t in range(8):
        W("t", "DVE", max(t_rqb, dve_p2.get(t - 1, 0)))
        op("t", lambda e, _t=t: e.matmul(bank(0), rqb[0:48, 128 * _t:128 * (_t + 1)], rkb[0:48, 0:512], start=True, stop=True))
        op("t", lambda e, _t=t: e.matmul(bank(1), rqb[0:48, 128 * _t:128 * (_t + 1)], rkb[0:48, 512:1024], start=True, stop=True))
        op("t", lambda e, _t=t: e.matmul(bank(2), queryb[0:64, 128 * _t:128 * (_t + 1)], akTs[0:64, 0:512], start=True, stop=True))
        t_p2 = op("t", lambda e, _t=t: e.matmul(bank(3), queryb[0:64, 128 * _t:128 * (_t + 1)], akTs[0:64, 512:1024], start=True, stop=True), inc="PE")
        W("v", "PE", t_p2)
        if t >= 1:
            W("v", "ACT", act_p2[t - 1])
        op("v", lambda e: e.scalar_tensor_tensor(s1p[:, :], ps[:, 0:1024], 1.0 / math.sqrt(RDIM), logrelB[:, :], ALU.mult, ALU.add))
        op("v", lambda e: e.scalar_tensor_tensor(s2p[:, :], ps[:, 1024:2048], 1.0 / math.sqrt(BOT), logrelB[:, :], ALU.mult, ALU.add))
        op("v", lambda e: e.max(m8[:, :], s1p[:, :]))
        op("v", lambda e: e.tensor_scalar(s1p[:, :], s1p[:, :], m8[:, 7:8], NEG, ALU.is_lt, ALU.mult))
        op("v", lambda e: e.tensor_add(s2p[:, :], s2p[:, :], s1p[:, :]))
        dve_p2[t] = op("v", lambda e: e.reduce_max(nmx[:, :], s2p[:, :], axis=AXX, negate=True), inc="DVE")
        W("s", "DVE", dve_p2[t])
        act_p2[t] = op("s", lambda e, _t=t: e.activation(
            e_all[:, 1024 * _t:1024 * (_t + 1)], s2p[:, :], AF.Exp,
            bias=nmx[:, :], accum_out=sums_all[:, _t:_t + 1]), inc="ACT")
    t_p2_act = act_p2[7]

    if lvl == 4:
        taps_src.update({"e_all": e_all[:, :], "sums_all": sums_all[:, :], "m8": m8[:, :]})
        return finish()

    wu1_t = {}

    def load_wu1(s_, kc, wait_pe=0):
        W("g", "PE", wait_pe)
        wu1_t[(s_, kc)] = dma("g", wu1b[kc % 2][:, :], wu1T_in[128 * kc:128 * (kc + 1), :], "LDG")

    load_wu1(0, 0)
    load_wu1(0, 1)
    t_sig = {}
    pe_wu1 = {}
    dve_gelu = {}
    for s_ in range(2):
        for kc in range(NKC):
            seq = 32 * s_ + kc
            tpar = seq % 2
            W("v", "PE", pe_wu1.get((0, NKC - 2 + kc), 0) if (s_ == 1 and kc < 2) else pe_wu1.get((s_, kc - 2), 0))
            t_cast = op("v", lambda e, _kc=kc, _s=s_, _p=tpar: e.tensor_copy(
                x1bf[_p][:, :], x1[:, 1024 * _kc + 512 * _s:1024 * _kc + 512 * _s + 512]), inc="DVE")
            W("t", "DVE", t_cast)
            W("t", "LDG", wu1_t[(s_, kc)])
            if kc == 0:
                if s_ == 0:
                    W("t", "ACT", t_p2_act)
                    W("t", "DVE", dve_p2[7])
                else:
                    W("t", "DVE", dve_gelu[(0, 7)])
                    W("t", "ACT", t_sig[0])
            for uc in range(8):
                def umm(e, _kc=kc, _uc=uc, _p=tpar):
                    return e.matmul(bank(_uc), wu1b[_kc % 2][:, 128 * _uc:128 * (_uc + 1)],
                                    x1bf[_p][:, :], start=(_kc == 0), stop=(_kc == NKC - 1))
                if uc == 7:
                    pe_wu1[(s_, kc)] = op("t", umm, inc="PE")
                else:
                    op("t", umm)
            nxt = (s_, kc + 2) if kc + 2 < NKC else ((1, kc + 2 - NKC) if s_ == 0 else None)
            if nxt:
                load_wu1(nxt[0], nxt[1], wait_pe=pe_wu1[(s_, kc)])
        xg, scp = gelu_sc[0], gelu_sc[1]
        W("v", "PE", pe_wu1[(s_, NKC - 1)])
        W("v", "LDS", t_bu1)
        for uc in range(8):
            t_xg = op("v", lambda e, _uc=uc: e.tensor_scalar(xg[:, :], bank(_uc), bu1s[:, _uc:_uc + 1], None, ALU.add), inc="DVE")
            W("s", "DVE", t_xg)
            t_sq2 = op("s", lambda e: e.activation(scp[:, :], xg[:, :], AF.Square), inc="ACT")
            W("v", "ACT", t_sq2)
            op("v", lambda e: e.tensor_scalar(scp[:, :], scp[:, :], GELU_B, 1.0, ALU.mult, ALU.add))
            t_arg = op("v", lambda e: e.tensor_mul(scp[:, :], scp[:, :], xg[:, :]), inc="DVE")
            W("s", "DVE", t_arg)
            t_th = op("s", lambda e: e.activation(scp[:, :], scp[:, :], AF.Tanh, scale=GELU_A), inc="ACT")
            W("v", "ACT", t_th)
            op("v", lambda e: e.tensor_scalar(scp[:, :], scp[:, :], 1.0, 0.5, ALU.add, ALU.mult))
            dve_gelu[(s_, uc)] = op("v", lambda e, _uc=uc: e.tensor_mul(
                gelu_b[:, 512 * _uc:512 * (_uc + 1)], scp[:, :], xg[:, :]), inc="DVE")
        W("t", "DVE", dve_gelu[(s_, 7)])
        W("t", "LDG", t_wu2)
        t_l = 0
        for uc in range(8):
            def lmm(e, _uc=uc):
                return e.matmul(ps[0:1, 0:512], wu2s[:, _uc:_uc + 1],
                                gelu_b[:, 512 * _uc:512 * (_uc + 1)], start=(_uc == 0), stop=(_uc == 7))
            if uc == 7:
                t_l = op("t", lmm, inc="PE")
            else:
                op("t", lmm)
        W("s", "PE", t_l)
        W("s", "LDS", t_scal)
        t_sig[s_] = op("s", lambda e, _s=s_: e.activation(
            sig2[0:1, 512 * _s:512 * _s + 512], ps[0:1, 0:512], AF.Sigmoid,
            bias=scal_sb[0:1, 0:1]), inc="ACT")

    if lvl == 5:
        taps_src.update({"sig2": sig2[:, :]})
        return finish()

    W("v", "LDS", t_lvg)
    W("v", "ACT", max(t_sig[0], t_sig[1]))
    op("v", lambda e: e.tensor_scalar(SC1, lvg[0:1, 0:1], 1.0 / (2 * S), 1e-6, ALU.mult, ALU.add))
    op("v", lambda e: e.reciprocal(SC2, SC1))
    op("v", lambda e: e.tensor_scalar(SC2, SC2, 0.5, None, ALU.mult))
    t_gatef = {}
    for s_, lv_t in ((0, lvA), (1, lvB)):
        gt_ = gtmp2[0:1, 512 * s_:512 * s_ + 512]
        sg_ = sig2[0:1, 512 * s_:512 * s_ + 512]
        gf_ = gatef[0:1, 512 * s_:512 * s_ + 512]
        op("v", lambda e, _g=gt_, _l=lv_t: e.tensor_scalar(_g, _l[0:1, :], SC2, None, ALU.mult))
        op("v", lambda e, _g=gt_, _sg=sg_: e.scalar_tensor_tensor(_g, _sg, 2.5, _g, ALU.mult, ALU.add))
        op("v", lambda e, _g=gt_: e.tensor_scalar(_g, _g, 5.0, 0.0, ALU.min, ALU.max))
        op("v", lambda e, _g=gt_: e.tensor_scalar(_g, _g, 0.5, 1.0 / 1.5, ALU.subtract, ALU.mult))
        t_un = op("v", lambda e, _g=gt_: e.tensor_scalar(_g, _g, 1.0, 0.0, ALU.min, ALU.max), inc="DVE")
        W("s", "DVE", t_un)
        t_g = op("s", lambda e, _g=gt_, _f=gf_: e.activation(
            _f, _g, AF.Sigmoid, bias=scal_sb[0:1, 2:3], scale=scal_sb[0:1, 1:2]), inc="ACT")
        W("v", "ACT", t_g)
        op("v", lambda e, _g=gt_, _f=gf_: e.tensor_scalar(_g, _f, 0.05, None, ALU.is_ge))
        t_gatef[s_] = op("v", lambda e, _g=gt_, _f=gf_: e.tensor_mul(_f, _f, _g), inc="DVE")
    W("t", "DVE", max(t_gatef[0], t_gatef[1]))
    t_gt = 0
    for t in range(8):
        def gtp(e, _t=t):
            return e.transpose(ps[:, 3584 + _t:3585 + _t], gatef[0:1, 128 * _t:128 * (_t + 1)], idf1[:, :])
        if t == 7:
            t_gt = op("t", gtp, inc="PE")
        else:
            op("t", gtp)
    W("v", "PE", t_gt)
    op("v", lambda e: e.tensor_copy(gate_t[:, :], ps[:, 3584:3592]))

    dve_wst_ev = {}
    pe_wst = {}
    for t in range(8):
        op("v", lambda e, _t=t: e.reciprocal(f_t[:, :], sums_all[:, _t:_t + 1]))
        op("v", lambda e, _t=t: e.tensor_mul(f_t[:, :], f_t[:, :], gate_t[:, _t:_t + 1]))
        t_wsn = op("v", lambda e, _t=t: e.tensor_scalar(
            e_all[:, 1024 * _t:1024 * (_t + 1)], e_all[:, 1024 * _t:1024 * (_t + 1)],
            f_t[:, :], None, ALU.mult), inc="DVE")
        W("t", "DVE", t_wsn)
        if t >= 1:
            W("t", "DVE", dve_wst_ev[t - 1])
        for sc in range(8):
            def wtp(e, _t=t, _sc=sc):
                return e.transpose(ps_bf[:, 6144 + 128 * _sc:6272 + 128 * _sc],
                                   e_all[:, 1024 * _t + 128 * _sc:1024 * _t + 128 * (_sc + 1)],
                                   identb[:, :])
            if sc == 7:
                pe_wst[t] = op("t", wtp, inc="PE")
            else:
                op("t", wtp)
        W("v", "PE", pe_wst[t])
        wdst = wsT.ap().rearrange("p (sc t2) -> p sc t2", sc=8)[:, :, 128 * t:128 * (t + 1)]
        wsrc = ps_bf[:, 6144:7168].rearrange("p (sc t2) -> p sc t2", sc=8)
        dve_wst_ev[t] = op("v", lambda e, _d=wdst, _s=wsrc: e.tensor_copy(_d, _s), inc="DVE")

    if lvl == 6:
        taps_src.update({"gate_t": gate_t[:, :], "wsT": wsT[:, :]})
        return finish()

    pe_aux2 = {}
    W("t", "DVE", dve_wst_ev[7])
    for v_ in range(2):
        for s_ in range(2):
            bk = 2 * v_ + s_
            for sc in range(8):
                def amm(e, _v=v_, _s=s_, _sc=sc, _bk=bk):
                    return e.matmul(bank(_bk), bauxp[:, 256 * _sc + 128 * _v:256 * _sc + 128 * _v + 128],
                                    wsT[:, 1024 * _sc + 512 * _s:1024 * _sc + 512 * _s + 512],
                                    start=(_sc == 0), stop=(_sc == 7))
                if sc == 7:
                    pe_aux2[(v_, s_)] = op("t", amm, inc="PE")
                else:
                    op("t", amm)
            W("v", "PE", pe_aux2[(v_, s_)])
            op("v", lambda e, _v=v_, _s=s_, _bk=bk: e.tensor_copy(
                aux2bf[:, 512 * (2 * _v + _s):512 * (2 * _v + _s) + 512], bank(_bk)), inc="DVE")
    t_aux2 = ctr["DVE"].n

    t_wvu = []
    for v_ in range(2):
        t_wvu.append(dma("g", wvuTs[:, 4096 * v_:4096 * (v_ + 1)],
                         wvuT_in[128 * v_:128 * (v_ + 1), :], "LDG"))
    pe_st2 = {}
    dve_x2 = {}
    W("t", "DVE", t_aux2)
    W("t", "LDG", t_wvu[1])
    for m in range(NKC):
        for s_ in range(2):
            if m >= 2:
                W("t", "DVE", dve_x2[(m - 2, s_)])
            t_a3 = 0
            for v_ in range(2):
                def a3mm(e, _v=v_, _m=m, _s=s_):
                    return e.matmul(bank(4 + (_m % 2)), wvuTs[:, 4096 * _v + 128 * _m:4096 * _v + 128 * _m + 128],
                                    aux2bf[:, 512 * (2 * _v + _s):512 * (2 * _v + _s) + 512],
                                    start=(_v == 0), stop=(_v == 1))
                if v_ == 1:
                    t_a3 = op("t", a3mm, inc="PE")
                else:
                    op("t", a3mm)
            x1sl = x1[:, 1024 * m + 512 * s_: 1024 * m + 512 * s_ + 512]
            W("v", "PE", t_a3)
            if m == 0 and s_ == 0:
                W("v", "PE", pe_aux2[(1, 1)])
            dve_x2[(m, s_)] = op("v", lambda e, _o=x1sl, _m=m: e.scalar_tensor_tensor(
                _o, _o, 2.0, bank(4 + (_m % 2)), ALU.mult, ALU.add), inc="DVE")
            W("v", "PE", pe_st2.get((m - 1, s_), 0))
            t_cast = op("v", lambda e, _o=x1sl, _s=s_: e.tensor_copy(x1bf[_s][:, :], _o), inc="DVE")
            W("s", "PE", pe_st2.get((m - 1, s_), 0))
            W("s", "DVE", dve_x2[(m, s_)])
            t_sq = op("s", lambda e, _o=x1sl, _s=s_: e.activation(xsqD[_s][:, :], _o, AF.Square), inc="ACT")
            W("t", "DVE", t_cast)
            W("t", "ACT", t_sq)
            op("t", lambda e, _m=m, _s=s_: e.matmul(
                ps[0:1, 512 * 2 * _s:512 * 2 * _s + 512], ones_bf[:, :],
                x1bf[_s][:, :], start=(_m == 0), stop=(_m == NKC - 1)))
            pe_st2[(m, s_)] = op("t", lambda e, _m=m, _s=s_: e.matmul(
                ps[0:1, 512 * (1 + 2 * _s):512 * (1 + 2 * _s) + 512], ones_bf[:, :],
                xsqD[_s][:, :], start=(_m == 0), stop=(_m == NKC - 1)), inc="PE")

    for s_ in range(2):
        mean_r = mn2[0:1, 512 * s_:512 * s_ + 512]
        rstd_r = rs2[0:1, 512 * s_:512 * s_ + 512]
        v_r = lvA[0:1, :] if s_ == 0 else lvB[0:1, :]
        W("v", "PE", pe_st2[(NKC - 1, s_)])
        op("v", lambda e, _m=mean_r, _s=s_: e.tensor_scalar(_m, ps[0:1, 512 * 2 * _s:512 * 2 * _s + 512], 1.0 / HID, None, ALU.mult))
        op("v", lambda e, _v=v_r, _s=s_: e.tensor_scalar(_v, ps[0:1, 512 * (1 + 2 * _s):512 * (1 + 2 * _s) + 512], 1.0 / HID, None, ALU.mult))
        op("v", lambda e, _r=rstd_r, _m=mean_r: e.tensor_mul(_r, _m, _m))
        t_var = op("v", lambda e, _v=v_r, _r=rstd_r: e.tensor_sub(_v, _v, _r), inc="DVE")
        W("s", "DVE", t_var)
        op("s", lambda e, _r=rstd_r, _v=v_r: e.activation(_r, _v, AF.Ln, bias=c_eps[0:1, 0:1]))
        t_rstd = op("s", lambda e, _r=rstd_r: e.activation(_r, _r, AF.Exp, scale=-0.5), inc="ACT")
        W("t", "ACT", t_rstd)
        if s_ == 1:
            W("t", "DVE", ln2_cp)
        op("t", lambda e, _m=mean_r: e.matmul(bank(6), onesf32[:, :], _m, start=True, stop=True))
        t_rb2 = op("t", lambda e, _r=rstd_r: e.matmul(bank(7), onesf32[:, :], _r, start=True, stop=True), inc="PE")
        W("v", "PE", t_rb2)
        op("v", lambda e: e.tensor_copy(mb_sb[:, :], bank(6)))
        t_cp2 = op("v", lambda e: e.tensor_copy(rstdb_sb[:, :], bank(7)), inc="DVE")
        ln2_cp = t_cp2
        for m in range(NKC):
            x1sl = x1[:, 1024 * m + 512 * s_: 1024 * m + 512 * s_ + 512]
            op("v", lambda e, _o=x1sl: e.tensor_sub(_o, _o, mb_sb[:, :]))
            t_ap = op("v", lambda e, _o=x1sl: e.tensor_mul(_o, _o, rstdb_sb[:, :]), inc="DVE")
            W("y", "DVE", t_ap)
            dma("y", y_out[128 * m:128 * (m + 1), 512 * s_:512 * (s_ + 1)], x1sl, "STS")

    taps_src.update({"e_all": e_all[:, :], "sums_all": sums_all[:, :],
                     "gate_t": gate_t[:, :], "queryb": queryb[:, :], "m8": m8[:, :]})
    return finish()


def _host_prep(I):
    x = np.ascontiguousarray(I["L0_attn_hidden_states"], dtype=np.float32)
    wqkvT = np.ascontiguousarray(np.concatenate(
        [np.asarray(I["L0_attn_Wk"]).T, np.asarray(I["L0_attn_Wv"]).T,
         np.asarray(I["L0_attn_Wq"]).T], axis=1), dtype=np.float32)
    inv_freq = (1.0 / 10000.0 ** (np.arange(0, HD, 2, dtype=np.float32) / HD)).astype(np.float32)
    tpos = np.arange(S, dtype=np.float32)
    freqs = np.outer(tpos, inv_freq).astype(np.float32)
    emb = np.concatenate([freqs, freqs], axis=-1)
    cos = np.cos(emb).astype(np.float32)
    sin = np.sin(emb).astype(np.float32)
    shared = dict(
        wqkvT=wqkvT,
        woT=np.ascontiguousarray(np.asarray(I["L0_attn_Wo"]).T, dtype=np.float32),
        ident_in=np.eye(128, dtype=np.float32),
        avT_in=np.ascontiguousarray(np.asarray(I["L0_ffn_aux_values"]).T, dtype=np.float32),
        wvdT_in=np.ascontiguousarray(np.asarray(I["L0_ffn_Wvd"]).T, dtype=np.float32),
        wvuT_in=np.ascontiguousarray(np.asarray(I["L0_ffn_Wvu"]).T, dtype=np.float32),
        wqfT_in=np.ascontiguousarray(np.asarray(I["L0_ffn_Wq"]).T, dtype=np.float32),
        wrT_in=np.ascontiguousarray(np.asarray(I["L0_ffn_Wr"]).T, dtype=np.float32),
        akT_in=np.ascontiguousarray(np.asarray(I["L0_ffn_aux_keys"]).T, dtype=np.float32),
        wu1T_in=np.ascontiguousarray(np.asarray(I["L0_ffn_Wu1"]).T, dtype=np.float32),
        wu2pp_in=np.ascontiguousarray(np.asarray(I["L0_ffn_Wu2"]).reshape(8, 128).T, dtype=np.float32),
        bu1pp_in=np.ascontiguousarray(np.asarray(I["L0_ffn_bu1"]).reshape(8, 128).T, dtype=np.float32),
        logrel_in=np.ascontiguousarray(np.asarray(I["L0_ffn_log_rel"]).reshape(1, NSLOT), dtype=np.float32),
    )
    scal = np.zeros((1, 16), dtype=np.float32)
    scal[0, 0] = np.float32(np.asarray(I["L0_ffn_bu2"]).reshape(-1)[0])
    scal[0, 1] = np.float32(np.asarray(I["L0_ffn_gate_w1"]))
    scal[0, 2] = np.float32(np.asarray(I["L0_ffn_gate_bias"]))
    shared["scal_in"] = scal

    in_maps, poslist = [], []
    for core in range(8):
        b, c = divmod(core, 4)
        pos = np.concatenate([np.arange(512 * g + 128 * c, 512 * g + 128 * c + 128)
                              for g in range(8)])
        poslist.append((b, pos))
        m = dict(shared)
        m["x_fm"] = np.ascontiguousarray(x[b, pos, :].T)
        m["cos_fm"] = np.ascontiguousarray(cos[pos, :].T)
        m["sin_fm"] = np.ascontiguousarray(sin[pos, :].T)
        cm = np.zeros((128, 2048), dtype=np.float32)
        for d in range(4):
            if d < c:
                tile = np.ones((128, 128), dtype=np.float32)
            elif d == c:
                tile = np.triu(np.ones((128, 128), dtype=np.float32))
            else:
                tile = np.zeros((128, 128), dtype=np.float32)
            cm[:, 512 * d:512 * (d + 1)] = np.tile(tile, (1, 4))
        m["cmask_in"] = cm
        in_maps.append(m)
    return in_maps, poslist


def kernel(**inputs):
    in_maps, poslist = _host_prep(inputs)
    nc = build_nc(upto="D")
    res = run_bass_kernel_spmd(nc, in_maps, core_ids=list(range(8)))
    out = np.empty((2, S, HID), dtype=np.float32)
    for core in range(8):
        b, pos = poslist[core]
        out[b, pos, :] = res.results[core]["y"].T
    return out
